# version 04
# baseline (speedup 1.0000x reference)
import numpy as np
import ml_dtypes
from contextlib import ExitStack
import concourse.bass as bass
import concourse.mybir as mybir
from concourse.bass_utils import run_bass_kernel_spmd

F32 = mybir.dt.float32
BF16 = mybir.dt.bfloat16
AF = mybir.ActivationFunctionType
OP = mybir.AluOpType
BFNP = ml_dtypes.bfloat16

DEBUG = False
STOP_AFTER = "all"
TAP_SINK = None

D = 1024
H = 16
S = 4096
OWN = 2048
EPS = 1e-5
MASKVAL = -50.0


class Sem:
    def __init__(self, h, name):
        self.h = h
        self.name = name
        self.n = 0

    def inc(self, k=1):
        self.n += k
        return self.n


class Prog:
    ENGINES = ("tensor", "vector", "scalar", "gpsimd", "sync")

    def __init__(self):
        self.steps = {e: [] for e in self.ENGINES}
        self.last_wait = {e: {} for e in self.ENGINES}

    def step(self, eng, emit, waits=(), incs=(), force=False):
        lw = self.last_wait[eng]
        real_waits = []
        for sm, v in waits:
            if v > 0 and lw.get(sm.name, 0) < v:
                real_waits.append((sm.h, v))
                lw[sm.name] = v
        if force and not real_waits:
            for sm, v in waits:
                if v > 0:
                    real_waits.append((sm.h, v))
                    break
        bound = []
        for sm, k in incs:
            sm.inc(k)
            bound.append((sm.h, k))

        def fn(e, _w=real_waits, _emit=emit, _b=bound):
            for h, v in _w:
                e.wait_ge(h, v)
            inst = _emit(e)
            if _b:
                if inst is None:
                    inst = e.nop()
                for h, k in _b:
                    inst = inst.then_inc(h, k)
        self.steps[eng].append(fn)

    def barrier(self, sems):
        targets = [(sm, sm.n) for sm in sems]
        for e in self.ENGINES:
            self.step(e, lambda _: None, waits=targets)


def build_nc():
    nc = bass.Bass("TRN2", target_bir_lowering=False)
    dt = lambda n, sh, dty, k: nc.dram_tensor(n, sh, dty, kind=k).ap()
    IN, OUT = "ExternalInput", "ExternalOutput"
    d_xkvT = dt("xkvT", [D, S], F32, IN)
    d_xresT = dt("xresT", [D, OWN], F32, IN)
    d_wq = dt("wq", [D, D], F32, IN)
    d_wk = dt("wk", [D, D], F32, IN)
    d_wv = dt("wv", [D, D], F32, IN)
    d_wo = dt("wo", [D, D], F32, IN)
    d_w1 = dt("w1", [D, 4 * D], F32, IN)
    d_w2 = dt("w2", [4 * D, D], F32, IN)
    d_bq = dt("bq", [128, 8], F32, IN)
    d_bk = dt("bk", [128, 8], F32, IN)
    d_bv = dt("bv", [128, 8], F32, IN)
    d_bvrep = dt("bvrep", [128, D], F32, IN)
    d_bo = dt("bo", [128, 8], F32, IN)
    d_b1 = dt("b1", [128, 32], F32, IN)
    d_b2 = dt("b2", [128, 8], F32, IN)
    d_btr = dt("btraw", [H, 128, 128], F32, IN)
    d_crep = dt("crep", [128, H], F32, IN)
    d_mask = dt("btmask", [128, 128], F32, IN)
    d_onesa = dt("ones_a", [128, 1], BF16, IN)
    d_onesc = dt("ones_c", [1, 128], BF16, IN)
    d_outT = dt("outT", [D, OWN], F32, OUT)
    taps = {}
    if DEBUG:
        taps["t_q"] = dt("t_q", [128, 4 * 2048], F32, OUT)
        taps["t_k"] = dt("t_k", [128, 4 * 4096], F32, OUT)
        taps["t_v"] = dt("t_v", [128, 32 * 520], F32, OUT)
        taps["t_aot"] = dt("t_aot", [128, 8 * 2048], F32, OUT)
        taps["t_wo"] = dt("t_wo", [128, 8 * 1024], F32, OUT)
        taps["t_x1"] = dt("t_x1", [128, 8 * 2048], F32, OUT)
        taps["t_ln2"] = dt("t_ln2", [128, 8 * 2048], F32, OUT)

    P = Prog()
    es = ExitStack()
    with es:
        sbt = lambda n, sh, dty: es.enter_context(nc.sbuf_tensor(n, sh, dty))
        pst = lambda n, sh: es.enter_context(nc.psum_tensor(n, sh, F32))
        mksem = lambda n: Sem(es.enter_context(nc.semaphore(n)), n)

        sT = mksem("sT")
        sV = mksem("sV")
        sS = mksem("sS")
        sG = mksem("sG")
        sX = mksem("sX")
        SEMS = (sT, sV, sS, sG, sX)

        SPa = pst("SPa", [128, 1024])
        SPb = pst("SPb", [128, 1024])
        POL0 = pst("POL0", [128, 512])
        POH0 = pst("POH0", [128, 512])
        POL1 = pst("POL1", [128, 512])
        POH1 = pst("POH1", [128, 512])

        bq8 = sbt("bq8", [128, 8], F32)
        bk_sb = sbt("bk_sb", [128, 8], F32)
        bo_sb = sbt("bo_sb", [128, 8], F32)
        bvrep_sb = sbt("bvrep_sb", [128, D], F32)
        b1_sb = sbt("b1_sb", [128, 32], F32)
        b2_sb = sbt("b2_sb", [128, 8], F32)
        onesa_sb = sbt("onesa_sb", [128, 1], BF16)
        onesc_sb = sbt("onesc_sb", [1, 128], BF16)
        bt_sb = sbt("bt_sb", [128, H * 128], BF16)
        m_sb = sbt("m_sb", [1, 512], F32)
        var_sb = sbt("var_sb", [1, 512], F32)
        sc1_sb = sbt("sc1_sb", [1, 512], F32)
        r_sb = sbt("r_sb", [1, 512], F32)
        bfrow = sbt("bfrow", [1, 3072], BF16)
        ohalf_sb = sbt("ohalf_sb", [128, 512], F32)
        xcb_sb = sbt("xcb_sb", [128, 2048], BF16)
        xcom = sbt("xcom", [128, 2 * 4096], BF16)
        et_sb = xcom
        rr0_sb, rr1_sb, sh_sb = m_sb, var_sb, sc1_sb
        rhi2 = bfrow[0:1, 1024:2048]
        rlo2 = bfrow[0:1, 2048:3072]

        AFI = AF.Identity

        def dma(eng_name, out_ap, in_ap, waits=()):
            sem = sG if eng_name == "gpsimd" else sX
            P.step(eng_name, lambda e, o=out_ap, i=in_ap: e.dma_start(out=o, in_=i),
                   waits=waits, incs=[(sem, 16)])

        P.step("scalar", lambda s: s.add_instruction(
            mybir.InstLoadActFuncSet(
                name=s.bass.get_next_instruction_name(),
                act_func_set_id=6, ins=[], outs=[])))

        open_stacks = []
        es_pro = ExitStack()
        btsg = es_pro.enter_context(nc.sbuf_tensor("btsg", [128, H * 128], F32))
        crep_sb = es_pro.enter_context(nc.sbuf_tensor("crep_sb", [128, H], F32))
        mask_sb = es_pro.enter_context(nc.sbuf_tensor("mask_sb", [128, 128], F32))
        for dsrc, dst in ((d_bq, bq8), (d_bk, bk_sb),
                          (d_bo, bo_sb), (d_b2, b2_sb)):
            dma("sync", dst[:], dsrc[:, :])
        dma("sync", bvrep_sb[:], d_bvrep[:, :])
        dma("sync", b1_sb[:], d_b1[:, :])
        dma("sync", crep_sb[:], d_crep[:, :])
        dma("sync", mask_sb[:], d_mask[:, :])
        dma("sync", onesa_sb[:], d_onesa[:, :])
        dma("sync", onesc_sb[:], d_onesc[:, :])
        dma("sync", btsg[:].rearrange("k (h q) -> k h q", h=H),
            d_btr.rearrange("h k q -> k h q"))
        pro_x = sX.n
        P.step("gpsimd", lambda g: g.memset(bfrow[0:1, 512:768], 0.0),
               incs=[(sG, 1)])
        P.step("vector", lambda v: v.tensor_scalar_mul(bq8[:], bq8[:], 0.125),
               waits=[(sX, pro_x)], incs=[(sV, 1)])
        for h in range(H):
            P.step("vector",
                   lambda v, h=h: v.scalar_tensor_tensor(
                       bt_sb[:, 128 * h:128 * h + 128],
                       btsg[:, 128 * h:128 * h + 128],
                       crep_sb[:, h:h + 1], mask_sb[:],
                       OP.subtract, OP.add),
                   waits=[(sX, pro_x)],
                   incs=([(sV, 1)] if h == H - 1 else []))
        es_pro.close()

        ln_state = {"i": 0, "norm": {}}

        def emit_ln_strip(src, t0, dst3, scal_guard):
            from_dram = not isinstance(src, tuple)
            i = ln_state["i"]
            ln_state["i"] += 1
            ev = (i % 2 == 0)
            PSst = POH0 if ev else POH1
            PSbc = POL0 if ev else POL1
            buf = i % 2
            x4 = xcom[:, 4096 * buf:4096 * buf + 4096].rearrange(
                "p (f c) -> p f c", c=512)
            xb3 = x4[:, :, 0:256]
            xq3 = x4[:, :, 256:512]
            free_v = ln_state["norm"].get(i - 2, 0)
            if from_dram:
                dma("gpsimd", xb3,
                    src.rearrange("(f p) t -> p f t", p=128)[:, :, t0:t0 + 256],
                    waits=[(sV, free_v)])
                ld = (sG, sG.n)
            else:
                x1_t, rel0 = src
                src3 = x1_t[:].rearrange("p (f c) -> p f c", f=8)[
                    :, :, rel0 + t0:rel0 + t0 + 256]
                P.step("vector", lambda v, s3=src3, x3=xb3: v.tensor_copy(x3, s3),
                       waits=[(sV, free_v)], incs=[(sV, 1)])
                ld = (sV, sV.n)
            P.step("gpsimd", lambda g, a=xq3, b=xb3: g.tensor_tensor(
                a, b, b, OP.mult), waits=[ld], incs=[(sG, 1)])
            sq_g = sG.n
            stat_free = ln_state["norm"].get(i - 2, 0)
            for f in range(8):
                P.step("tensor", lambda t, f=f, PS=PSst, buf=buf: t.matmul(
                    PS[0:1, 0:512], onesa_sb[:],
                    xcom[:, 4096 * buf + 512 * f:4096 * buf + 512 * f + 512],
                    start=(f == 0), stop=(f == 7)),
                    waits=[(sG, sq_g), (sV, stat_free), scal_guard],
                    incs=([(sT, 1)] if f == 7 else []),
                    force=(f == 0))
            stat_t = sT.n
            P.step("vector", lambda v, PS=PSst: v.tensor_scalar_mul(
                m_sb[:, 0:256], PS[0:1, 0:256], 1.0 / D), waits=[(sT, stat_t)])
            P.step("vector", lambda v, PS=PSst: v.tensor_scalar(
                var_sb[:, 0:256], PS[0:1, 256:512], 1.0 / D, EPS,
                OP.mult, OP.add))
            P.step("vector", lambda v: v.tensor_mul(
                sc1_sb[:, 0:256], m_sb[:, 0:256], m_sb[:, 0:256]))
            P.step("vector", lambda v: v.tensor_sub(
                var_sb[:, 0:256], var_sb[:, 0:256], sc1_sb[:, 0:256]),
                incs=[(sV, 1)])
            st_v = sV.n
            P.step("scalar", lambda s: s.activation(
                sc1_sb[:, 0:256], var_sb[:, 0:256], AF.Ln),
                waits=[(sV, st_v)])
            P.step("scalar", lambda s: s.activation(
                r_sb[:, 0:256], sc1_sb[:, 0:256], AF.Exp, scale=-0.5),
                incs=[(sS, 1)])
            r_s = sS.n
            P.step("vector", lambda v: v.tensor_copy(
                bfrow[0:1, 256:512], r_sb[:, 0:256]), waits=[(sS, r_s)])
            P.step("vector", lambda v: v.tensor_sub(
                bfrow[0:1, 768:1024], r_sb[:, 0:256], bfrow[0:1, 256:512]))
            P.step("vector", lambda v: v.tensor_copy(
                bfrow[0:1, 0:256], m_sb[:, 0:256]), incs=[(sV, 1)])
            hl_v = sV.n
            P.step("tensor", lambda t, PS=PSbc: t.matmul(
                PS[:, 0:512], onesc_sb[0:1, :], bfrow[0:1, 0:512],
                start=True, stop=False),
                waits=[(sV, hl_v), (sV, stat_free), (sG, 1), scal_guard],
                force=True)
            P.step("tensor", lambda t, PS=PSbc: t.matmul(
                PS[:, 0:512], onesc_sb[0:1, :], bfrow[0:1, 512:1024],
                start=False, stop=True),
                incs=[(sT, 1)])
            bc_t = sT.n
            xc3 = xcb_sb[:].rearrange("p (f c) -> p f c", f=8)
            mb_b = PSbc[:, 0:256].rearrange("p (f c) -> p f c", f=1).broadcast_to(
                [128, 8, 256])
            rb_b = PSbc[:, 256:512].rearrange("p (f c) -> p f c", f=1).broadcast_to(
                [128, 8, 256])
            P.step("vector", lambda v, x3=xb3, mb=mb_b: v.tensor_tensor(
                xc3, x3, mb, OP.subtract), waits=[(sT, bc_t)])
            P.step("vector", lambda v, rb=rb_b, d3=dst3: v.tensor_tensor(
                d3, xc3, rb, OP.mult), incs=[(sV, 1)])
            ln_state["norm"][i] = sV.n
            return sV.n

        es_aot = ExitStack()
        open_stacks.append(es_aot)
        aot = es_aot.enter_context(nc.sbuf_tensor("aot", [128, 8 * 2048], BF16))
        es_p1 = ExitStack()
        open_stacks.append(es_p1)
        p1 = lambda n, sh, dty: es_p1.enter_context(nc.sbuf_tensor(n, sh, dty))
        qb = p1("qb", [128, 4 * 2048], BF16)
        kb = p1("kb", [128, 4 * 4096], BF16)
        vb = p1("vb", [128, 32 * 520], BF16)
        lnt = p1("lnt", [128, 2 * 4096], BF16)
        wqk = p1("wqk", [128, 4 * 2048], BF16)
        wv_sb = p1("wv_sb", [128, 8 * 512], BF16)

        vcnt = {"n": 0, "ev": {}}
        qkcnt = {"n": 0, "ev": {}}

        def emit_proj_unit(u, lnt_v, wv_l, wqk_l, maybe_ln):
            Hh, E, j = u // 8, u % 8, u % 2
            lbase = 4096 * j
            bvr = bvrep_sb[:, 512 * Hh:512 * Hh + 512].rearrange(
                "p (h c) -> p h c", c=64)
            for tt in range(4):
                if tt == 2:
                    maybe_ln()
                tglob = 4 * E + tt
                n = vcnt["n"]
                vcnt["n"] += 1
                PS = SPa if n % 2 == 0 else SPb
                for f in range(8):
                    P.step("tensor", lambda t, f=f, tt=tt, PS=PS, lb=lbase:
                           t.matmul(
                               PS[:, 0:512],
                               lnt[:, lb + 512 * f + 128 * tt:
                                   lb + 512 * f + 128 * tt + 128],
                               wv_sb[:, 512 * f:512 * f + 512],
                               start=(f == 0), stop=(f == 7)),
                           waits=[(sV, lnt_v), (sG, wv_l),
                                  (sV, vcnt["ev"].get(n - 2, 0))],
                           incs=([(sT, 1)] if f == 7 else []),
                           force=(f == 0))
                mm_t = sT.n
                dstv = vb[:, 520 * tglob:520 * tglob + 520].rearrange(
                    "p (h c) -> p h c", c=65)[:, :, 0:64]
                P.step("vector", lambda v, d=dstv, PS=PS, bvr=bvr: v.tensor_tensor(
                    d, PS[:, 0:512].rearrange("p (h c) -> p h c", c=64), bvr,
                    OP.add),
                    waits=[(sT, mm_t)], incs=[(sV, 1)])
                vcnt["ev"][n] = sV.n
            for kind in range(2):
                for g4 in range(4):
                    if kind == 1 and g4 == 0:
                        maybe_ln()
                    gg = 4 * Hh + g4
                    n = qkcnt["n"]
                    qkcnt["n"] += 1
                    PS = SPa if n % 2 == 0 else SPb
                    if kind == 0:
                        w = wqk[:, 2048 * g4:2048 * g4 + 1024]
                        for f in range(8):
                            rhs = lnt[:, lbase + 512 * f:
                                      lbase + 512 * f + 512].rearrange(
                                "p (r c) -> p r c", c=128)[:, :, 0:64]
                            P.step("tensor",
                                   lambda t, f=f, PS=PS, w=w, rhs=rhs: t.matmul(
                                       PS[:, 512:768], w[:, 128 * f:128 * f + 128],
                                       rhs, start=(f == 0), stop=(f == 7)),
                                   waits=[(sV, lnt_v), (sG, wqk_l),
                                          (sS, qkcnt["ev"].get(n - 2, 0))],
                                   incs=([(sT, 1)] if f == 7 else []),
                                   force=(f == 0))
                        mm_t = sT.n
                        dcol = 2048 * g4 + 256 * E
                        P.step("scalar",
                               lambda s, PS=PS, dcol=dcol, gg=gg: s.activation(
                                   qb[:, dcol:dcol + 256], PS[:, 512:768], AFI,
                                   bias=bq8[:, gg:gg + 1], scale=0.125),
                               waits=[(sT, mm_t)], incs=[(sS, 1)])
                        qkcnt["ev"][n] = sS.n
                    else:
                        w = wqk[:, 2048 * g4 + 1024:2048 * g4 + 2048]
                        for f in range(8):
                            P.step("tensor",
                                   lambda t, f=f, PS=PS, w=w, lb=lbase: t.matmul(
                                       PS[:, 512:1024],
                                       w[:, 128 * f:128 * f + 128],
                                       lnt[:, lb + 512 * f:lb + 512 * f + 512],
                                       start=(f == 0), stop=(f == 7)),
                                   waits=[(sV, lnt_v), (sG, wqk_l),
                                          (sS, qkcnt["ev"].get(n - 2, 0))],
                                   incs=([(sT, 1)] if f == 7 else []),
                                   force=(f == 0))
                        mm_t = sT.n
                        dcol = 4096 * g4 + 512 * E
                        P.step("scalar",
                               lambda s, PS=PS, dcol=dcol, gg=gg: s.activation(
                                   kb[:, dcol:dcol + 512], PS[:, 512:1024], AFI,
                                   bias=bk_sb[:, gg:gg + 1]),
                               waits=[(sT, mm_t)], incs=[(sS, 1)])
                        qkcnt["ev"][n] = sS.n
            return

        def emit_attention(Hh):
            prev_evac = sV.n
            prev_lastexp = sS.n
            OL = [POL0, POL1]
            OH = [POH0, POH1]
            for g4 in range(4):
                gg = 4 * Hh + g4
                for s_ in range(4):
                    A = 8 * s_ + 8
                    sT_scores = {}
                    sS_exp = {}
                    sV_bias = {}
                    sT_av = {}

                    def emit_av(a, start, stop, s_=s_, g4=g4, sS_exp=sS_exp,
                                sT_av=sT_av, pe=prev_evac, pl=prev_lastexp):
                        j = a % 2
                        qs = max(64 * a, 512 * s_)
                        ql = 512 * (s_ + 1) - qs
                        off = qs - 512 * s_
                        ET = et_sb[:, 1024 * j:1024 * j + 1024]
                        w1_ = [(sS, sS_exp[a])]
                        if a == 0:
                            w1_ = w1_ + [(sV, pe), (sS, pl)]
                        vcol = 520 * a + 65 * (2 * g4)
                        for hh in range(2):
                            vc = vcol + 65 * hh
                            for half in range(2):
                                r0 = 64 * half
                                OPS = (OL if half == 0 else OH)[hh]
                                P.step("tensor",
                                       lambda t, OPS=OPS, vc=vc, r0=r0, hh=hh,
                                       ET=ET, off=off, ql=ql,
                                       start=start, stop=stop: t.matmul(
                                           OPS[0:65, off:off + ql],
                                           vb[r0:r0 + 64, vc:vc + 65],
                                           ET[r0:r0 + 64, 512 * hh:512 * hh + ql],
                                           start=start, stop=stop),
                                       waits=w1_,
                                       incs=([(sT, 1)] if (hh == 1 and half == 1)
                                             else []))
                        sT_av[a] = sT.n

                    for a in range(A):
                        j = a % 2
                        SP = SPa if j == 0 else SPb
                        qs = max(64 * a, 512 * s_)
                        qe = 512 * (s_ + 1)
                        ql = qe - qs
                        if a >= 2:
                            w_ = [(sS, sS_exp[a - 2])]
                        elif a == 1 or (g4 == 0 and s_ == 0):
                            w_ = [(sS, prev_lastexp), (sV, prev_evac)]
                        else:
                            w_ = [(sS, prev_lastexp)]
                        for hh in range(2):
                            r0 = 64 * hh
                            P.step("tensor",
                                   lambda t, SP=SP, r0=r0, hh=hh, a=a, g4=g4,
                                   qs=qs, ql=ql: t.matmul(
                                       SP[:, 512 * hh:512 * hh + ql],
                                       kb[r0:r0 + 64, 4096 * g4 + 128 * a:
                                          4096 * g4 + 128 * a + 128],
                                       qb[r0:r0 + 64, 2048 * g4 + qs:
                                          2048 * g4 + qs + ql],
                                       start=True, stop=True),
                                   waits=w_,
                                   incs=([(sT, 1)] if hh == 1 else []))
                        sT_scores[a] = sT.n
                        need_bias = (64 * a + 128 > qs)
                        if need_bias:
                            btc0 = 64 if 64 * a < qs else 0
                            L = 64 if btc0 == 64 else min(128, qe - 64 * a)
                            spv = SP[:].rearrange(
                                "p (h c) -> p h c", c=512)[:, :, 0:L]
                            btv = bt_sb[:, 256 * gg:256 * gg + 256].rearrange(
                                "p (h c) -> p h c", c=128)[:, :, btc0:btc0 + L]
                            wb_ = [(sT, sT_scores[a]),
                                   (sS, sS_exp[a - 2]) if a >= 2
                                   else (sS, prev_lastexp)]
                            P.step("vector",
                                   lambda v, spv=spv, btv=btv: v.tensor_tensor(
                                       spv, spv, btv, OP.add),
                                   waits=wb_, incs=[(sV, 1)])
                            sV_bias[a] = sV.n
                        ET = et_sb[:, 1024 * j:1024 * j + 1024]
                        if ql == 512:
                            srcE = SP[:, 0:1024]
                            dstE = ET[:, 0:1024]
                        else:
                            srcE = SP[:].rearrange(
                                "p (h c) -> p h c", c=512)[:, :, 0:ql]
                            dstE = ET[:].rearrange(
                                "p (h c) -> p h c", c=512)[:, :, 0:ql]
                        we_ = ([(sV, sV_bias[a])] if need_bias
                               else [(sT, sT_scores[a])])
                        P.step("scalar",
                               lambda s, dstE=dstE, srcE=srcE: s.activation(
                                   dstE, srcE, AF.Exp),
                               waits=we_, incs=[(sS, 1)])
                        sS_exp[a] = sS.n
                        if a >= 1:
                            emit_av(a - 1, start=(a - 1 == 0), stop=False)
                    emit_av(A - 1, start=False, stop=True)
                    av_t = sT_av[A - 1]
                    P.step("vector", lambda v: v.tensor_copy(
                        sh_sb[:], POL0[64:65, 0:512]), waits=[(sT, av_t)])
                    P.step("vector", lambda v: v.tensor_tensor(
                        rr0_sb[:], sh_sb[:], POH0[64:65, 0:512], OP.add))
                    P.step("vector", lambda v: v.reciprocal(rr0_sb[:], rr0_sb[:]))
                    P.step("vector", lambda v: v.tensor_copy(
                        sh_sb[:], POL1[64:65, 0:512]))
                    P.step("vector", lambda v: v.tensor_tensor(
                        rr1_sb[:], sh_sb[:], POH1[64:65, 0:512], OP.add))
                    P.step("vector", lambda v: v.reciprocal(rr1_sb[:], rr1_sb[:]))
                    P.step("vector", lambda v: v.tensor_copy(
                        rhi2[0:1, 0:512], rr0_sb[:]))
                    P.step("vector", lambda v: v.tensor_sub(
                        rlo2[0:1, 0:512], rr0_sb[:], rhi2[0:1, 0:512]))
                    P.step("vector", lambda v: v.tensor_copy(
                        rhi2[0:1, 512:1024], rr1_sb[:]))
                    P.step("vector", lambda v: v.tensor_sub(
                        rlo2[0:1, 512:1024], rr1_sb[:], rhi2[0:1, 512:1024]),
                        incs=[(sV, 1)])
                    sums_v = sV.n
                    for hh in range(2):
                        P.step("tensor", lambda t, hh=hh: t.matmul(
                            SPb[0:64, 512 * hh:512 * hh + 512],
                            onesc_sb[0:1, 0:64],
                            rhi2[0:1, 512 * hh:512 * hh + 512],
                            start=True, stop=False),
                            waits=[(sV, sums_v)])
                        P.step("tensor", lambda t, hh=hh: t.matmul(
                            SPb[0:64, 512 * hh:512 * hh + 512],
                            onesc_sb[0:1, 0:64],
                            rlo2[0:1, 512 * hh:512 * hh + 512],
                            start=False, stop=True),
                            incs=([(sT, 1)] if hh == 1 else []))
                    bc_t = sT.n
                    for hh in range(2):
                        OLh = OL[hh]
                        OHh = OH[hh]
                        P.step("vector", lambda v, OHh=OHh: v.tensor_copy(
                            ohalf_sb[0:64, :], OHh[0:64, 0:512]),
                            waits=[(sT, bc_t)])
                        P.step("vector", lambda v, OLh=OLh: v.tensor_tensor(
                            ohalf_sb[0:64, :], ohalf_sb[0:64, :],
                            OLh[0:64, 0:512], OP.add))
                        acol = 2048 * gg + 512 * s_
                        P.step("vector",
                               lambda v, hh=hh, acol=acol: v.tensor_tensor(
                                   aot[64 * hh:64 * hh + 64, acol:acol + 512],
                                   ohalf_sb[0:64, :],
                                   SPb[0:64, 512 * hh:512 * hh + 512], OP.mult),
                               incs=([(sV, 1)] if hh == 1 else []))
                    prev_evac = sV.n
                    prev_lastexp = sS_exp[A - 1]

        lnt_v = {}
        lnst = {"next_u": 0, "next_k": 0}

        def maybe_ln():
            u2 = lnst["next_u"]
            if u2 > 15:
                return
            E2, j2 = u2 % 8, u2 % 2
            k = lnst["next_k"]
            t0 = 512 * E2 + 256 * k
            dst3 = lnt[:].rearrange("p (b c) -> p b c", c=512)[
                :, 8 * j2:8 * j2 + 8, 256 * k:256 * k + 256]
            v = emit_ln_strip(d_xkvT, t0, dst3, (sS, sS.n))
            if k == 1:
                lnt_v[u2] = v
                lnst["next_u"] += 1
                lnst["next_k"] = 0
            else:
                lnst["next_k"] = 1

        wv_l = {}
        wqk_l = {}

        def load_w(Hh):
            dma("gpsimd", wv_sb[:].rearrange("p (f c) -> p f c", f=8),
                d_wv.rearrange("(f p) c -> p f c", p=128)[
                    :, :, 512 * Hh:512 * Hh + 512],
                waits=[(sT, sT.n)])
            wv_l[Hh] = sG.n
            for g4 in range(4):
                gg = 4 * Hh + g4
                dma("gpsimd",
                    wqk[:, 2048 * g4:2048 * g4 + 1024].rearrange(
                        "p (f c) -> p f c", f=8),
                    d_wq.rearrange("(f p) c -> p f c", p=128)[
                        :, :, 128 * gg:128 * gg + 128],
                    waits=[(sT, sT.n)])
                dma("gpsimd",
                    wqk[:, 2048 * g4 + 1024:2048 * g4 + 2048].rearrange(
                        "p (f c) -> p f c", f=8),
                    d_wk.rearrange("(f p) c -> p f c", p=128)[
                        :, :, 128 * gg:128 * gg + 128])
            wqk_l[Hh] = sG.n

        load_w(0)
        maybe_ln()
        maybe_ln()
        for u in range(16):
            Hh = u // 8
            emit_proj_unit(u, lnt_v[u], wv_l[Hh], wqk_l[Hh], maybe_ln)
            if u == 7:
                if DEBUG:
                    P.barrier(SEMS)
                    dma("gpsimd", taps["t_q"][:, :], qb[:])
                    dma("gpsimd", taps["t_k"][:, :], kb[:])
                    dma("gpsimd", taps["t_v"][:, :], vb[:])
                    P.barrier(SEMS)
                if STOP_AFTER == "proj":
                    break
                load_w(1)
            if u in (7, 15) and STOP_AFTER != "proj":
                P.step("gpsimd", lambda g: g.memset(
                    vb[:].rearrange("p (t h c) -> p t h c", h=8, c=65)[
                        :, :, :, 64:65],
                    1.0), waits=[(sS, sS.n), (sV, sV.n)], incs=[(sG, 1)])
                P.barrier(SEMS)
                emit_attention(u // 8)
                P.barrier(SEMS)
        es_p1.close()
        open_stacks.remove(es_p1)

        if STOP_AFTER not in ("proj", "attn"):
            es_x1 = ExitStack()
            open_stacks.append(es_x1)
            x1t = es_x1.enter_context(
                nc.sbuf_tensor("x1t", [128, 8 * 2048], F32, side="right"))
            es_ln2 = ExitStack()
            open_stacks.append(es_ln2)
            ln2t = es_ln2.enter_context(
                nc.sbuf_tensor("ln2t", [128, 8 * 2048], BF16, side="right"))
            es_wo = ExitStack()
            open_stacks.append(es_wo)
            wo_sb = es_wo.enter_context(
                nc.sbuf_tensor("wo_sb", [128, 8 * 1024], BF16, side="right"))
            dma("gpsimd", wo_sb[:].rearrange("p (f c) -> p f c", f=8),
                d_wo.rearrange("(f p) c -> p f c", p=128))
            wo_l = sG.n
            dma("sync", x1t[:].rearrange("p (f c) -> p f c", f=8),
                d_xresT.rearrange("(f p) t -> p f t", p=128))
            xres_l = sX.n
            bo2_v = 0

            PO4 = [(SPa, 0), (SPa, 512), (SPb, 0), (SPb, 512)]
            o_evac = {}
            ln2v = 0
            k = 0
            for ts in range(4):
                for e in range(8):
                    PS, co = PO4[k % 4]
                    for g in range(8):
                        P.step("tensor",
                               lambda t, PS=PS, co=co, e=e, g=g, ts=ts: t.matmul(
                                   PS[:, co:co + 512],
                                   wo_sb[:, 1024 * g + 128 * e:
                                         1024 * g + 128 * e + 128],
                                   aot[:, 2048 * g + 512 * ts:
                                       2048 * g + 512 * ts + 512],
                                   start=(g == 0), stop=(g == 7)),
                               waits=[(sG, wo_l), (sV, o_evac.get(k - 4, 0)),
                                      (sX, pro_x)],
                               incs=([(sT, 1)] if g == 7 else []),
                               force=(g == 0))
                    mm_t = sT.n
                    dcol = 2048 * e + 512 * ts
                    P.step("vector",
                           lambda v, PS=PS, co=co, dcol=dcol, e=e:
                           v.scalar_tensor_tensor(
                               x1t[:, dcol:dcol + 512], PS[:, co:co + 512],
                               bo_sb[:, e:e + 1], x1t[:, dcol:dcol + 512],
                               OP.add, OP.add),
                           waits=[(sT, mm_t), (sX, xres_l)], incs=[(sV, 1)])
                    o_evac[k] = sV.n
                    k += 1
                if STOP_AFTER != "oproj":
                    for k8 in (2 * ts, 2 * ts + 1):
                        t0 = 256 * k8
                        d3 = ln2t[:].rearrange("p (f c) -> p f c", c=2048)[
                            :, :, t0:t0 + 256]
                        ln2v = emit_ln_strip((x1t, 0), t0, d3, (sS, sS.n))
            P.barrier(SEMS)
            if DEBUG:
                dma("gpsimd", taps["t_aot"][:, :], aot[:])
                dma("sync", taps["t_x1"][:, :], x1t[:])
                dma("gpsimd", taps["t_wo"][:, :], wo_sb[:])
                P.barrier(SEMS)
            es_wo.close()
            open_stacks.remove(es_wo)
            es_aot.close()
            open_stacks.remove(es_aot)

            if STOP_AFTER != "oproj":
                for f in range(8):
                    P.step("vector", lambda v, f=f: v.tensor_scalar_add(
                        x1t[:, 2048 * f:2048 * f + 2048],
                        x1t[:, 2048 * f:2048 * f + 2048], b2_sb[:, f:f + 1]),
                        waits=[(sV, ln2v)],
                        incs=([(sV, 1)] if f == 7 else []))
                P.barrier(SEMS)
                if DEBUG:
                    dma("gpsimd", taps["t_ln2"][:, :], ln2t[:]),
                    P.barrier(SEMS)

                if STOP_AFTER != "ln2":
                    es_ffn = ExitStack()
                    open_stacks.append(es_ffn)
                    w1_sb = es_ffn.enter_context(
                        nc.sbuf_tensor("w1_sb", [128, 8 * 1024], BF16,
                                       side="right"))
                    w2_sb = es_ffn.enter_context(
                        nc.sbuf_tensor("w2_sb", [128, 8 * 1024], BF16,
                                       side="right"))
                    ht = es_ffn.enter_context(
                        nc.sbuf_tensor("ht", [128, 8 * 2048], BF16, side="right"))
                    PSLOT = [(SPa, 0), (SPa, 512), (SPb, 0), (SPb, 512),
                             (POL0, 0), (POH0, 0), (POL1, 0), (POH1, 0)]
                    h_done_t = 0
                    out_done_t = 0
                    out_evac_all = 0
                    for q in range(4):
                        dma("gpsimd", w1_sb[:].rearrange("p (f c) -> p f c", f=8),
                            d_w1.rearrange("(f p) c -> p f c", p=128)[
                                :, :, 1024 * q:1024 * q + 1024],
                            waits=[(sT, h_done_t)])
                        w1q_l = sG.n
                        dma("gpsimd", w2_sb[:].rearrange("p (f c) -> p f c", f=8),
                            d_w2.rearrange("(f p) c -> p f c", p=128)[
                                :, 8 * q:8 * q + 8, :],
                            waits=[(sT, out_done_t)])
                        w2q_l = sG.n
                        h_evac = {}
                        for m in range(8):
                            slots = PSLOT[0:4] if m % 2 == 0 else PSLOT[4:8]
                            for f in range(8):
                                for ts in range(4):
                                    PS, co = slots[ts]
                                    P.step("tensor",
                                           lambda t, PS=PS, co=co, f=f, m=m, ts=ts:
                                           t.matmul(
                                               PS[:, co:co + 512],
                                               w1_sb[:, 1024 * f + 128 * m:
                                                     1024 * f + 128 * m + 128],
                                               ln2t[:, 2048 * f + 512 * ts:
                                                    2048 * f + 512 * ts + 512],
                                               start=(f == 0), stop=(f == 7)),
                                           waits=[(sG, w1q_l),
                                                  (sS, h_evac.get(m - 2, 0)),
                                                  (sV, out_evac_all)],
                                           incs=([(sT, 1)]
                                                 if (f == 7 and ts == 3) else []),
                                           force=(ts == 0))
                            mm_t = sT.n
                            for ts in range(4):
                                PS, co = slots[ts]
                                P.step("scalar",
                                       lambda s, PS=PS, co=co, m=m, q=q, ts=ts:
                                       s.activation(
                                           ht[:, 2048 * m + 512 * ts:
                                              2048 * m + 512 * ts + 512],
                                           PS[:, co:co + 512], AF.Relu,
                                           bias=b1_sb[:, 8 * q + m:8 * q + m + 1]),
                                       waits=[(sT, mm_t), (sT, out_done_t)],
                                       incs=([(sS, 1)] if ts == 3 else []))
                            h_evac[m] = sS.n
                        h_done_t = sT.n
                        h_all_s = sS.n
                        oev = {}
                        for e in range(8):
                            slots = PSLOT[0:4] if e % 2 == 0 else PSLOT[4:8]
                            for hid in range(8):
                                for ts in range(4):
                                    PS, co = slots[ts]
                                    P.step("tensor",
                                           lambda t, PS=PS, co=co, e=e, hid=hid,
                                           ts=ts: t.matmul(
                                               PS[:, co:co + 512],
                                               w2_sb[:, 1024 * hid + 128 * e:
                                                     1024 * hid + 128 * e + 128],
                                               ht[:, 2048 * hid + 512 * ts:
                                                  2048 * hid + 512 * ts + 512],
                                               start=(hid == 0), stop=(hid == 7)),
                                           waits=[(sG, w2q_l), (sS, h_all_s),
                                                  (sV, oev.get(e - 2, 0))],
                                           incs=([(sT, 1)]
                                                 if (hid == 7 and ts == 3) else []),
                                           force=(ts == 0))
                            mm_t = sT.n
                            for ts in range(4):
                                PS, co = slots[ts]
                                dcol = 2048 * e + 512 * ts
                                P.step("vector",
                                       lambda v, PS=PS, co=co, dcol=dcol:
                                       v.tensor_tensor(
                                           x1t[:, dcol:dcol + 512],
                                           x1t[:, dcol:dcol + 512],
                                           PS[:, co:co + 512], OP.add),
                                       waits=[(sT, mm_t)],
                                       incs=([(sV, 1)] if ts == 3 else []))
                            oev[e] = sV.n
                        out_done_t = sT.n
                        out_evac_all = sV.n
                    es_ffn.close()
                    open_stacks.remove(es_ffn)
                    es_ln2.close()
                    open_stacks.remove(es_ln2)
            P.barrier(SEMS)
            dma("sync", d_outT.rearrange("(f p) t -> p f t", p=128),
                x1t[:].rearrange("p (f c) -> p f c", f=8))
        else:
            with nc.sbuf_tensor("zro", [128, 2048], F32, side="right") as zro:
                P.step("vector", lambda v: v.memset(zro[:], 0.0), incs=[(sV, 1)])
                zv = sV.n
                for f in range(8):
                    dma("sync", d_outT[128 * f:128 * f + 128, :], zro[:, :],
                        waits=[(sV, zv)])
        for st_ in reversed(open_stacks):
            try:
                st_.close()
            except Exception:
                pass
        P.step("sync", lambda e: None, waits=[(sX, sX.n), (sG, sG.n)])

        blk = es.enter_context(nc.Block())

        def _run(name):
            def body(eng, _n=name):
                for fn in P.steps[_n]:
                    fn(eng)
            return body
        blk.tensor(_run("tensor"))
        blk.vector(_run("vector"))
        blk.scalar(_run("scalar"))
        blk.gpsimd(_run("gpsimd"))
        blk.sync(_run("sync"))
    return nc


def _host_tables(p, rel_bias):
    qi = np.arange(64)
    dmap = ({(0, 0): 0, (0, 1): 2, (1, 0): None, (1, 1): 1} if p == 0 else
            {(0, 0): 0, (0, 1): 2, (1, 0): 1, (1, 1): 3})
    idx = np.zeros((128, 128), np.int64)
    for jh in (0, 1):
        for rh in (0, 1):
            d = dmap[(jh, rh)]
            if d is None:
                blk = np.full((64, 64), 256, np.int64)
            else:
                blk = np.clip(64 * d + qi[None, :] - qi[:, None], -128, 128) + 128
            idx[64 * jh:64 * jh + 64, 64 * rh:64 * rh + 64] = blk
    btraw = np.ascontiguousarray(rel_bias[idx].transpose(2, 0, 1)).astype(np.float32)
    mask = np.zeros((128, 128), np.float32)
    if p == 0:
        mask[64:128, 0:64] = MASKVAL
    return btraw, mask


def kernel(L0_attn_x, L0_attn_Wq, L0_attn_bq, L0_attn_Wk, L0_attn_bk,
           L0_attn_Wv, L0_attn_bv, L0_attn_Wo, L0_attn_bo, L0_attn_rel_bias,
           L0_ffn_w1, L0_ffn_b1, L0_ffn_w2, L0_ffn_b2):
    f32 = np.float32
    x = np.asarray(L0_attn_x, f32)
    rel = np.asarray(L0_attn_rel_bias, f32)
    shared = {
        "wq": np.ascontiguousarray(L0_attn_Wq, f32),
        "wk": np.ascontiguousarray(L0_attn_Wk, f32),
        "wv": np.ascontiguousarray(L0_attn_Wv, f32),
        "wo": np.ascontiguousarray(L0_attn_Wo, f32),
        "w1": np.ascontiguousarray(L0_ffn_w1, f32),
        "w2": np.ascontiguousarray(L0_ffn_w2, f32),
        "bq": np.ascontiguousarray(np.asarray(L0_attn_bq, f32).reshape(8, 128).T),
        "bk": np.ascontiguousarray(np.asarray(L0_attn_bk, f32).reshape(8, 128).T),
        "bv": np.ascontiguousarray(np.asarray(L0_attn_bv, f32).reshape(8, 128).T),
        "bvrep": np.ascontiguousarray(
            np.broadcast_to(np.asarray(L0_attn_bv, f32)[None, :], (128, D))),
        "bo": np.ascontiguousarray(np.asarray(L0_attn_bo, f32).reshape(8, 128).T),
        "b1": np.ascontiguousarray(np.asarray(L0_ffn_b1, f32).reshape(32, 128).T),
        "b2": np.ascontiguousarray(np.asarray(L0_ffn_b2, f32).reshape(8, 128).T),
        "crep": np.ascontiguousarray(
            np.broadcast_to(rel[256][None, :], (128, H)).astype(f32)),
        "ones_a": np.ones((128, 1), BFNP),
        "ones_c": np.ones((1, 128), BFNP),
    }
    tables = {p: _host_tables(p, rel) for p in (0, 1)}
    in_maps = []
    for c in range(8):
        b, p = c >> 1, c & 1
        ch = x[b].reshape(64, 64, D)
        order = [j ^ p for j in range(64)]
        xkvT = np.ascontiguousarray(ch[order].reshape(S, D).T)
        own = [2 * r + p for r in range(32)]
        xresT = np.ascontiguousarray(ch[own].reshape(OWN, D).T)
        im = dict(shared)
        im["xkvT"] = xkvT
        im["xresT"] = xresT
        im["btraw"] = tables[p][0]
        im["btmask"] = tables[p][1]
        in_maps.append(im)
    nc = build_nc()
    res = run_bass_kernel_spmd(nc, in_maps, core_ids=list(range(8)))
    if TAP_SINK is not None:
        TAP_SINK.append(res.results)
    out = np.empty((4, S, D), f32)
    for c in range(8):
        b, p = c >> 1, c & 1
        oT = np.asarray(res.results[c]["outT"], f32)
        out[b].reshape(64, 64, D)[[2 * r + p for r in range(32)]] = (
            oT.T.reshape(32, 64, D))
    return out


# version 05
# speedup vs baseline: 1.0093x; 1.0093x over previous
import numpy as np
import ml_dtypes
from contextlib import ExitStack
import concourse.bass as bass
import concourse.mybir as mybir
from concourse.bass_utils import run_bass_kernel_spmd

F32 = mybir.dt.float32
BF16 = mybir.dt.bfloat16
AF = mybir.ActivationFunctionType
OP = mybir.AluOpType
BFNP = ml_dtypes.bfloat16

DEBUG = False
STOP_AFTER = "all"
TAP_SINK = None

D = 1024
H = 16
S = 4096
OWN = 2048
EPS = 1e-5
MASKVAL = -50.0


class Sem:
    def __init__(self, h, name):
        self.h = h
        self.name = name
        self.n = 0

    def inc(self, k=1):
        self.n += k
        return self.n


class Prog:
    ENGINES = ("tensor", "vector", "scalar", "gpsimd", "sync")

    def __init__(self):
        self.steps = {e: [] for e in self.ENGINES}
        self.last_wait = {e: {} for e in self.ENGINES}

    def step(self, eng, emit, waits=(), incs=(), force=False):
        lw = self.last_wait[eng]
        real_waits = []
        for sm, v in waits:
            if v > 0 and lw.get(sm.name, 0) < v:
                real_waits.append((sm.h, v))
                lw[sm.name] = v
        if force and not real_waits:
            for sm, v in waits:
                if v > 0:
                    real_waits.append((sm.h, v))
                    break
        bound = []
        for sm, k in incs:
            sm.inc(k)
            bound.append((sm.h, k))

        def fn(e, _w=real_waits, _emit=emit, _b=bound):
            for h, v in _w:
                e.wait_ge(h, v)
            inst = _emit(e)
            if _b:
                if inst is None:
                    inst = e.nop()
                for h, k in _b:
                    inst = inst.then_inc(h, k)
        self.steps[eng].append(fn)

    def barrier(self, sems):
        targets = [(sm, sm.n) for sm in sems]
        for e in self.ENGINES:
            self.step(e, lambda _: None, waits=targets)


def build_nc():
    nc = bass.Bass("TRN2", target_bir_lowering=False)
    dt = lambda n, sh, dty, k: nc.dram_tensor(n, sh, dty, kind=k).ap()
    IN, OUT = "ExternalInput", "ExternalOutput"
    d_xkvT = dt("xkvT", [D, S], F32, IN)
    d_xresT = dt("xresT", [D, OWN], F32, IN)
    d_wq = dt("wq", [D, D], F32, IN)
    d_wk = dt("wk", [D, D], F32, IN)
    d_wv = dt("wv", [D, D], F32, IN)
    d_wo = dt("wo", [D, D], F32, IN)
    d_w1 = dt("w1", [D, 4 * D], F32, IN)
    d_w2 = dt("w2", [4 * D, D], F32, IN)
    d_bq = dt("bq", [128, 8], F32, IN)
    d_bk = dt("bk", [128, 8], F32, IN)
    d_bv = dt("bv", [128, 8], F32, IN)
    d_bvrep = dt("bvrep", [128, D], F32, IN)
    d_bo = dt("bo", [128, 8], F32, IN)
    d_b1 = dt("b1", [128, 32], F32, IN)
    d_b2 = dt("b2", [128, 8], F32, IN)
    d_btr = dt("btraw", [H, 128, 128], F32, IN)
    d_crep = dt("crep", [128, H], F32, IN)
    d_mask = dt("btmask", [128, 128], F32, IN)
    d_onesa = dt("ones_a", [128, 1], BF16, IN)
    d_onesc = dt("ones_c", [1, 128], BF16, IN)
    d_outT = dt("outT", [D, OWN], F32, OUT)
    taps = {}
    if DEBUG:
        taps["t_q"] = dt("t_q", [128, 4 * 2048], F32, OUT)
        taps["t_k"] = dt("t_k", [128, 4 * 4096], F32, OUT)
        taps["t_v"] = dt("t_v", [128, 32 * 520], F32, OUT)
        taps["t_aot"] = dt("t_aot", [128, 8 * 2048], F32, OUT)
        taps["t_wo"] = dt("t_wo", [128, 8 * 1024], F32, OUT)
        taps["t_x1"] = dt("t_x1", [128, 8 * 2048], F32, OUT)
        taps["t_ln2"] = dt("t_ln2", [128, 8 * 2048], F32, OUT)

    P = Prog()
    es = ExitStack()
    with es:
        sbt = lambda n, sh, dty: es.enter_context(nc.sbuf_tensor(n, sh, dty))
        pst = lambda n, sh: es.enter_context(nc.psum_tensor(n, sh, F32))
        mksem = lambda n: Sem(es.enter_context(nc.semaphore(n)), n)

        sT = mksem("sT")
        sV = mksem("sV")
        sS = mksem("sS")
        sG = mksem("sG")
        sX = mksem("sX")
        SEMS = (sT, sV, sS, sG, sX)

        SPa = pst("SPa", [128, 1024])
        SPb = pst("SPb", [128, 1024])
        SPc = pst("SPc", [128, 1024])
        POa = pst("POa", [128, 512])
        POb = pst("POb", [128, 512])
        SLOT = (SPa, SPb, SPc)

        bq8 = sbt("bq8", [128, 8], F32)
        bk_sb = sbt("bk_sb", [128, 8], F32)
        bo_sb = sbt("bo_sb", [128, 8], F32)
        bvrep_sb = sbt("bvrep_sb", [128, D], F32)
        b1_sb = sbt("b1_sb", [128, 32], F32)
        b2_sb = sbt("b2_sb", [128, 8], F32)
        onesa_sb = sbt("onesa_sb", [128, 1], BF16)
        onesc_sb = sbt("onesc_sb", [1, 128], BF16)
        bt_sb = sbt("bt_sb", [128, H * 128], BF16)
        m_sb = sbt("m_sb", [1, 512], F32)
        var_sb = sbt("var_sb", [1, 512], F32)
        sc1_sb = sbt("sc1_sb", [1, 512], F32)
        r_sb = sbt("r_sb", [1, 512], F32)
        bfrow = sbt("bfrow", [1, 3072], BF16)
        ohalf_sb = sbt("ohalf_sb", [128, 512], F32)
        xcb_sb = sbt("xcb_sb", [128, 2048], BF16)
        xcom = sbt("xcom", [128, 2 * 4096], BF16)
        et_sb = xcom
        rr0_sb, rr1_sb, sh_sb = m_sb, var_sb, sc1_sb
        rhi2 = bfrow[0:1, 1024:2048]
        rlo2 = bfrow[0:1, 2048:3072]

        AFI = AF.Identity

        def dma(eng_name, out_ap, in_ap, waits=()):
            sem = sG if eng_name == "gpsimd" else sX
            P.step(eng_name, lambda e, o=out_ap, i=in_ap: e.dma_start(out=o, in_=i),
                   waits=waits, incs=[(sem, 16)])

        P.step("scalar", lambda s: s.add_instruction(
            mybir.InstLoadActFuncSet(
                name=s.bass.get_next_instruction_name(),
                act_func_set_id=6, ins=[], outs=[])))

        open_stacks = []
        es_pro = ExitStack()
        btsg = es_pro.enter_context(nc.sbuf_tensor("btsg", [128, H * 128], F32))
        crep_sb = es_pro.enter_context(nc.sbuf_tensor("crep_sb", [128, H], F32))
        mask_sb = es_pro.enter_context(nc.sbuf_tensor("mask_sb", [128, 128], F32))
        for dsrc, dst in ((d_bq, bq8), (d_bk, bk_sb),
                          (d_bo, bo_sb), (d_b2, b2_sb)):
            dma("sync", dst[:], dsrc[:, :])
        dma("sync", bvrep_sb[:], d_bvrep[:, :])
        dma("sync", b1_sb[:], d_b1[:, :])
        dma("sync", crep_sb[:], d_crep[:, :])
        dma("sync", mask_sb[:], d_mask[:, :])
        dma("sync", onesa_sb[:], d_onesa[:, :])
        dma("sync", onesc_sb[:], d_onesc[:, :])
        dma("sync", btsg[:].rearrange("k (h q) -> k h q", h=H),
            d_btr.rearrange("h k q -> k h q"))
        pro_x = sX.n
        P.step("gpsimd", lambda g: g.memset(bfrow[0:1, 512:768], 0.0),
               incs=[(sG, 1)])
        P.step("vector", lambda v: v.tensor_scalar_mul(bq8[:], bq8[:], 0.125),
               waits=[(sX, pro_x)], incs=[(sV, 1)])
        for h in range(H):
            P.step("vector",
                   lambda v, h=h: v.scalar_tensor_tensor(
                       bt_sb[:, 128 * h:128 * h + 128],
                       btsg[:, 128 * h:128 * h + 128],
                       crep_sb[:, h:h + 1], mask_sb[:],
                       OP.subtract, OP.add),
                   waits=[(sX, pro_x)],
                   incs=([(sV, 1)] if h == H - 1 else []))
        es_pro.close()

        ln_state = {"i": 0, "norm": {}}

        def emit_ln_strip(src, t0, dst3, scal_guard):
            from_dram = not isinstance(src, tuple)
            i = ln_state["i"]
            ln_state["i"] += 1
            ev = (i % 2 == 0)
            PSst = POa if ev else POb
            PSbc = (SPc[:, 0:512] if ev else SPc[:, 512:1024])
            buf = i % 2
            x4 = xcom[:, 4096 * buf:4096 * buf + 4096].rearrange(
                "p (f c) -> p f c", c=512)
            xb3 = x4[:, :, 0:256]
            xq3 = x4[:, :, 256:512]
            free_v = ln_state["norm"].get(i - 2, 0)
            if from_dram:
                dma("gpsimd", xb3,
                    src.rearrange("(f p) t -> p f t", p=128)[:, :, t0:t0 + 256],
                    waits=[(sV, free_v)])
                ld = (sG, sG.n)
            else:
                x1_t, rel0 = src
                src3 = x1_t[:].rearrange("p (f c) -> p f c", f=8)[
                    :, :, rel0 + t0:rel0 + t0 + 256]
                P.step("vector", lambda v, s3=src3, x3=xb3: v.tensor_copy(x3, s3),
                       waits=[(sV, free_v)], incs=[(sV, 1)])
                ld = (sV, sV.n)
            P.step("gpsimd", lambda g, a=xq3, b=xb3: g.tensor_tensor(
                a, b, b, OP.mult), waits=[ld], incs=[(sG, 1)])
            sq_g = sG.n
            stat_free = ln_state["norm"].get(i - 2, 0)
            for f in range(8):
                P.step("tensor", lambda t, f=f, PS=PSst, buf=buf: t.matmul(
                    PS[0:1, 0:512], onesa_sb[:],
                    xcom[:, 4096 * buf + 512 * f:4096 * buf + 512 * f + 512],
                    start=(f == 0), stop=(f == 7)),
                    waits=[(sG, sq_g), (sV, stat_free), scal_guard],
                    incs=([(sT, 1)] if f == 7 else []),
                    force=(f == 0))
            stat_t = sT.n
            P.step("vector", lambda v, PS=PSst: v.tensor_scalar_mul(
                m_sb[:, 0:256], PS[0:1, 0:256], 1.0 / D), waits=[(sT, stat_t)])
            P.step("vector", lambda v, PS=PSst: v.tensor_scalar(
                var_sb[:, 0:256], PS[0:1, 256:512], 1.0 / D, EPS,
                OP.mult, OP.add))
            P.step("vector", lambda v: v.tensor_mul(
                sc1_sb[:, 0:256], m_sb[:, 0:256], m_sb[:, 0:256]))
            P.step("vector", lambda v: v.tensor_sub(
                var_sb[:, 0:256], var_sb[:, 0:256], sc1_sb[:, 0:256]),
                incs=[(sV, 1)])
            st_v = sV.n
            P.step("scalar", lambda s: s.activation(
                sc1_sb[:, 0:256], var_sb[:, 0:256], AF.Ln),
                waits=[(sV, st_v)])
            P.step("scalar", lambda s: s.activation(
                r_sb[:, 0:256], sc1_sb[:, 0:256], AF.Exp, scale=-0.5),
                incs=[(sS, 1)])
            r_s = sS.n
            P.step("vector", lambda v: v.tensor_copy(
                bfrow[0:1, 256:512], r_sb[:, 0:256]), waits=[(sS, r_s)])
            P.step("vector", lambda v: v.tensor_sub(
                bfrow[0:1, 768:1024], r_sb[:, 0:256], bfrow[0:1, 256:512]))
            P.step("vector", lambda v: v.tensor_copy(
                bfrow[0:1, 0:256], m_sb[:, 0:256]), incs=[(sV, 1)])
            hl_v = sV.n
            P.step("tensor", lambda t, PS=PSbc: t.matmul(
                PS[:, 0:512], onesc_sb[0:1, :], bfrow[0:1, 0:512],
                start=True, stop=False),
                waits=[(sV, hl_v), (sV, stat_free), (sG, 1), scal_guard],
                force=True)
            P.step("tensor", lambda t, PS=PSbc: t.matmul(
                PS[:, 0:512], onesc_sb[0:1, :], bfrow[0:1, 512:1024],
                start=False, stop=True),
                incs=[(sT, 1)])
            bc_t = sT.n
            xc3 = xcb_sb[:].rearrange("p (f c) -> p f c", f=8)
            mb_b = PSbc[:, 0:256].rearrange("p (f c) -> p f c", f=1).broadcast_to(
                [128, 8, 256])
            rb_b = PSbc[:, 256:512].rearrange("p (f c) -> p f c", f=1).broadcast_to(
                [128, 8, 256])
            P.step("vector", lambda v, x3=xb3, mb=mb_b: v.tensor_tensor(
                xc3, x3, mb, OP.subtract), waits=[(sT, bc_t)])
            P.step("vector", lambda v, rb=rb_b, d3=dst3: v.tensor_tensor(
                d3, xc3, rb, OP.mult), incs=[(sV, 1)])
            ln_state["norm"][i] = sV.n
            return sV.n

        es_aot = ExitStack()
        open_stacks.append(es_aot)
        aot = es_aot.enter_context(nc.sbuf_tensor("aot", [128, 8 * 2048], BF16))
        es_p1 = ExitStack()
        open_stacks.append(es_p1)
        p1 = lambda n, sh, dty: es_p1.enter_context(nc.sbuf_tensor(n, sh, dty))
        qb = p1("qb", [128, 4 * 2048], BF16)
        kb = p1("kb", [128, 4 * 4096], BF16)
        vb = p1("vb", [128, 32 * 520], BF16)
        lnt = p1("lnt", [128, 2 * 4096], BF16)
        wqk = p1("wqk", [128, 4 * 2048], BF16)
        wv_sb = p1("wv_sb", [128, 8 * 512], BF16)

        vcnt = {"n": 0, "ev": {}}
        qkcnt = {"n": 0, "ev": {}}

        def emit_proj_unit(u, lnt_v, wv_l, wqk_l, maybe_ln):
            Hh, E, j = u // 8, u % 8, u % 2
            lbase = 4096 * j
            bvr = bvrep_sb[:, 512 * Hh:512 * Hh + 512].rearrange(
                "p (h c) -> p h c", c=64)
            for tt in range(4):
                if tt == 2:
                    maybe_ln()
                tglob = 4 * E + tt
                n = vcnt["n"]
                vcnt["n"] += 1
                PS = SPa if n % 2 == 0 else SPb
                for f in range(8):
                    P.step("tensor", lambda t, f=f, tt=tt, PS=PS, lb=lbase:
                           t.matmul(
                               PS[:, 0:512],
                               lnt[:, lb + 512 * f + 128 * tt:
                                   lb + 512 * f + 128 * tt + 128],
                               wv_sb[:, 512 * f:512 * f + 512],
                               start=(f == 0), stop=(f == 7)),
                           waits=[(sV, lnt_v), (sG, wv_l),
                                  (sV, vcnt["ev"].get(n - 2, 0))],
                           incs=([(sT, 1)] if f == 7 else []),
                           force=(f == 0))
                mm_t = sT.n
                dstv = vb[:, 520 * tglob:520 * tglob + 520].rearrange(
                    "p (h c) -> p h c", c=65)[:, :, 0:64]
                P.step("vector", lambda v, d=dstv, PS=PS, bvr=bvr: v.tensor_tensor(
                    d, PS[:, 0:512].rearrange("p (h c) -> p h c", c=64), bvr,
                    OP.add),
                    waits=[(sT, mm_t)], incs=[(sV, 1)])
                vcnt["ev"][n] = sV.n
            for kind in range(2):
                for g4 in range(4):
                    if kind == 1 and g4 == 0:
                        maybe_ln()
                    gg = 4 * Hh + g4
                    n = qkcnt["n"]
                    qkcnt["n"] += 1
                    PS = SPa if n % 2 == 0 else SPb
                    if kind == 0:
                        w = wqk[:, 2048 * g4:2048 * g4 + 1024]
                        for f in range(8):
                            rhs = lnt[:, lbase + 512 * f:
                                      lbase + 512 * f + 512].rearrange(
                                "p (r c) -> p r c", c=128)[:, :, 0:64]
                            P.step("tensor",
                                   lambda t, f=f, PS=PS, w=w, rhs=rhs: t.matmul(
                                       PS[:, 512:768], w[:, 128 * f:128 * f + 128],
                                       rhs, start=(f == 0), stop=(f == 7)),
                                   waits=[(sV, lnt_v), (sG, wqk_l),
                                          (sS, qkcnt["ev"].get(n - 2, 0))],
                                   incs=([(sT, 1)] if f == 7 else []),
                                   force=(f == 0))
                        mm_t = sT.n
                        dcol = 2048 * g4 + 256 * E
                        P.step("scalar",
                               lambda s, PS=PS, dcol=dcol, gg=gg: s.activation(
                                   qb[:, dcol:dcol + 256], PS[:, 512:768], AFI,
                                   bias=bq8[:, gg:gg + 1], scale=0.125),
                               waits=[(sT, mm_t)], incs=[(sS, 1)])
                        qkcnt["ev"][n] = sS.n
                    else:
                        w = wqk[:, 2048 * g4 + 1024:2048 * g4 + 2048]
                        for f in range(8):
                            P.step("tensor",
                                   lambda t, f=f, PS=PS, w=w, lb=lbase: t.matmul(
                                       PS[:, 512:1024],
                                       w[:, 128 * f:128 * f + 128],
                                       lnt[:, lb + 512 * f:lb + 512 * f + 512],
                                       start=(f == 0), stop=(f == 7)),
                                   waits=[(sV, lnt_v), (sG, wqk_l),
                                          (sS, qkcnt["ev"].get(n - 2, 0))],
                                   incs=([(sT, 1)] if f == 7 else []),
                                   force=(f == 0))
                        mm_t = sT.n
                        dcol = 4096 * g4 + 512 * E
                        P.step("scalar",
                               lambda s, PS=PS, dcol=dcol, gg=gg: s.activation(
                                   kb[:, dcol:dcol + 512], PS[:, 512:1024], AFI,
                                   bias=bk_sb[:, gg:gg + 1]),
                               waits=[(sT, mm_t)], incs=[(sS, 1)])
                        qkcnt["ev"][n] = sS.n
            return

        def emit_attention(Hh):
            prev_evac = sV.n
            slot_exp = {0: sS.n, 1: sS.n, 2: sS.n}
            slot_vec = {0: prev_evac, 1: prev_evac, 2: prev_evac}
            prev_lastexp = sS.n
            PO = (POa, POb)
            for g4 in range(4):
                gg = 4 * Hh + g4
                for s_ in range(4):
                    A = 8 * s_ + 8
                    sT_scores = {}
                    sS_exp = {}
                    sV_bias = {}
                    sT_av = {}

                    def emit_av(a, start, stop, s_=s_, g4=g4, sS_exp=sS_exp,
                                sT_av=sT_av, pe=prev_evac, pl=prev_lastexp):
                        j = a % 3
                        qs = max(64 * a, 512 * s_)
                        ql = 512 * (s_ + 1) - qs
                        off = qs - 512 * s_
                        ET = xcom[:, 1024 * j:1024 * j + 1024]
                        w1_ = [(sS, sS_exp[a])]
                        if a == 0:
                            w1_ = w1_ + [(sV, pe), (sS, pl)]
                        vcol = 520 * a + 65 * (2 * g4)
                        for hh in range(2):
                            vc = vcol + 65 * hh
                            P.step("tensor",
                                   lambda t, OPS=PO[hh], vc=vc, hh=hh,
                                   ET=ET, off=off, ql=ql,
                                   start=start, stop=stop: t.matmul(
                                       OPS[0:65, off:off + ql],
                                       vb[0:128, vc:vc + 65],
                                       ET[0:128, 512 * hh:512 * hh + ql],
                                       start=start, stop=stop),
                                   waits=w1_,
                                   incs=([(sT, 1)] if hh == 1 else []))
                        sT_av[a] = sT.n

                    for a in range(A):
                        j = a % 3
                        SP = SLOT[j]
                        qs = max(64 * a, 512 * s_)
                        qe = 512 * (s_ + 1)
                        ql = qe - qs
                        if a >= 3:
                            w_ = [(sS, sS_exp[a - 3])]
                        else:
                            w_ = [(sS, slot_exp[j]), (sV, slot_vec[j])]
                        for hh in range(2):
                            r0 = 64 * hh
                            P.step("tensor",
                                   lambda t, SP=SP, r0=r0, hh=hh, a=a, g4=g4,
                                   qs=qs, ql=ql: t.matmul(
                                       SP[:, 512 * hh:512 * hh + ql],
                                       kb[r0:r0 + 64, 4096 * g4 + 128 * a:
                                          4096 * g4 + 128 * a + 128],
                                       qb[r0:r0 + 64, 2048 * g4 + qs:
                                          2048 * g4 + qs + ql],
                                       start=True, stop=True),
                                   waits=w_,
                                   incs=([(sT, 1)] if hh == 1 else []),
                                   force=(hh == 0))
                        sT_scores[a] = sT.n
                        need_bias = (64 * a + 128 > qs)
                        if need_bias:
                            btc0 = 64 if 64 * a < qs else 0
                            L = 64 if btc0 == 64 else min(128, qe - 64 * a)
                            spv = SP[:].rearrange(
                                "p (h c) -> p h c", c=512)[:, :, 0:L]
                            btv = bt_sb[:, 256 * gg:256 * gg + 256].rearrange(
                                "p (h c) -> p h c", c=128)[:, :, btc0:btc0 + L]
                            wb_ = [(sT, sT_scores[a]),
                                   (sS, sS_exp[a - 3]) if a >= 3
                                   else (sS, slot_exp[j])]
                            P.step("vector",
                                   lambda v, spv=spv, btv=btv: v.tensor_tensor(
                                       spv, spv, btv, OP.add),
                                   waits=wb_, incs=[(sV, 1)])
                            sV_bias[a] = sV.n
                        ET = xcom[:, 1024 * j:1024 * j + 1024]
                        if ql == 512:
                            srcE = SP[:, 0:1024]
                            dstE = ET[:, 0:1024]
                        else:
                            srcE = SP[:].rearrange(
                                "p (h c) -> p h c", c=512)[:, :, 0:ql]
                            dstE = ET[:].rearrange(
                                "p (h c) -> p h c", c=512)[:, :, 0:ql]
                        we_ = ([(sV, sV_bias[a])] if need_bias
                               else [(sT, sT_scores[a])])
                        P.step("scalar",
                               lambda s, dstE=dstE, srcE=srcE: s.activation(
                                   dstE, srcE, AF.Exp),
                               waits=we_, incs=[(sS, 1)])
                        sS_exp[a] = sS.n
                        if a >= 1:
                            emit_av(a - 1, start=(a - 1 == 0), stop=False)
                    emit_av(A - 1, start=False, stop=True)
                    av_t = sT_av[A - 1]
                    P.step("vector", lambda v: v.tensor_copy(
                        sh_sb[:], POa[64:65, 0:512]), waits=[(sT, av_t)])
                    P.step("vector", lambda v: v.reciprocal(
                        rr0_sb[:], sh_sb[:]))
                    P.step("vector", lambda v: v.tensor_copy(
                        sh_sb[:], POb[64:65, 0:512]))
                    P.step("vector", lambda v: v.reciprocal(
                        rr1_sb[:], sh_sb[:]))
                    P.step("vector", lambda v: v.tensor_copy(
                        rhi2[0:1, 0:512], rr0_sb[:]))
                    P.step("vector", lambda v: v.tensor_sub(
                        rlo2[0:1, 0:512], rr0_sb[:], rhi2[0:1, 0:512]))
                    P.step("vector", lambda v: v.tensor_copy(
                        rhi2[0:1, 512:1024], rr1_sb[:]))
                    P.step("vector", lambda v: v.tensor_sub(
                        rlo2[0:1, 512:1024], rr1_sb[:], rhi2[0:1, 512:1024]),
                        incs=[(sV, 1)])
                    sums_v = sV.n
                    L3 = (A - 1) % 3
                    SPL = SLOT[L3]
                    for hh in range(2):
                        P.step("tensor", lambda t, hh=hh, SPL=SPL: t.matmul(
                            SPL[0:64, 512 * hh:512 * hh + 512],
                            onesc_sb[0:1, 0:64],
                            rhi2[0:1, 512 * hh:512 * hh + 512],
                            start=True, stop=False),
                            waits=[(sV, sums_v)], force=True)
                        P.step("tensor", lambda t, hh=hh, SPL=SPL: t.matmul(
                            SPL[0:64, 512 * hh:512 * hh + 512],
                            onesc_sb[0:1, 0:64],
                            rlo2[0:1, 512 * hh:512 * hh + 512],
                            start=False, stop=True),
                            incs=([(sT, 1)] if hh == 1 else []))
                    bc_t = sT.n
                    for hh in range(2):
                        P.step("vector", lambda v, hh=hh: v.tensor_copy(
                            ohalf_sb[0:64, :], PO[hh][0:64, 0:512]),
                            waits=[(sT, bc_t)])
                        acol = 2048 * gg + 512 * s_
                        P.step("vector",
                               lambda v, hh=hh, acol=acol, SPL=SPL: v.tensor_tensor(
                                   aot[64 * hh:64 * hh + 64, acol:acol + 512],
                                   ohalf_sb[0:64, :],
                                   SPL[0:64, 512 * hh:512 * hh + 512], OP.mult),
                               incs=([(sV, 1)] if hh == 1 else []))
                    prev_evac = sV.n
                    prev_lastexp = sS_exp[A - 1]
                    for j3 in range(3):
                        lastu = A - 1 - ((A - 1 - j3) % 3)
                        slot_exp[j3] = sS_exp[lastu]
                        slot_vec[j3] = prev_evac if j3 == L3 else slot_vec[j3]
                    slot_vec[L3] = prev_evac

        lnt_v = {}
        lnst = {"next_u": 0, "next_k": 0}

        def maybe_ln():
            u2 = lnst["next_u"]
            if u2 > 15:
                return
            E2, j2 = u2 % 8, u2 % 2
            k = lnst["next_k"]
            t0 = 512 * E2 + 256 * k
            dst3 = lnt[:].rearrange("p (b c) -> p b c", c=512)[
                :, 8 * j2:8 * j2 + 8, 256 * k:256 * k + 256]
            v = emit_ln_strip(d_xkvT, t0, dst3, (sS, sS.n))
            if k == 1:
                lnt_v[u2] = v
                lnst["next_u"] += 1
                lnst["next_k"] = 0
            else:
                lnst["next_k"] = 1

        wv_l = {}
        wqk_l = {}

        def load_w(Hh):
            dma("gpsimd", wv_sb[:].rearrange("p (f c) -> p f c", f=8),
                d_wv.rearrange("(f p) c -> p f c", p=128)[
                    :, :, 512 * Hh:512 * Hh + 512],
                waits=[(sT, sT.n)])
            wv_l[Hh] = sG.n
            for g4 in range(4):
                gg = 4 * Hh + g4
                dma("gpsimd",
                    wqk[:, 2048 * g4:2048 * g4 + 1024].rearrange(
                        "p (f c) -> p f c", f=8),
                    d_wq.rearrange("(f p) c -> p f c", p=128)[
                        :, :, 128 * gg:128 * gg + 128],
                    waits=[(sT, sT.n)])
                dma("gpsimd",
                    wqk[:, 2048 * g4 + 1024:2048 * g4 + 2048].rearrange(
                        "p (f c) -> p f c", f=8),
                    d_wk.rearrange("(f p) c -> p f c", p=128)[
                        :, :, 128 * gg:128 * gg + 128])
            wqk_l[Hh] = sG.n

        load_w(0)
        maybe_ln()
        maybe_ln()
        for u in range(16):
            Hh = u // 8
            emit_proj_unit(u, lnt_v[u], wv_l[Hh], wqk_l[Hh], maybe_ln)
            if u == 7:
                if DEBUG:
                    P.barrier(SEMS)
                    dma("gpsimd", taps["t_q"][:, :], qb[:])
                    dma("gpsimd", taps["t_k"][:, :], kb[:])
                    dma("gpsimd", taps["t_v"][:, :], vb[:])
                    P.barrier(SEMS)
                if STOP_AFTER == "proj":
                    break
                load_w(1)
            if u in (7, 15) and STOP_AFTER != "proj":
                P.step("gpsimd", lambda g: g.memset(
                    vb[:].rearrange("p (t h c) -> p t h c", h=8, c=65)[
                        :, :, :, 64:65],
                    1.0), waits=[(sS, sS.n), (sV, sV.n)], incs=[(sG, 1)])
                P.barrier(SEMS)
                emit_attention(u // 8)
                P.barrier(SEMS)
        es_p1.close()
        open_stacks.remove(es_p1)

        if STOP_AFTER not in ("proj", "attn"):
            es_x1 = ExitStack()
            open_stacks.append(es_x1)
            x1t = es_x1.enter_context(
                nc.sbuf_tensor("x1t", [128, 8 * 2048], F32, side="right"))
            es_ln2 = ExitStack()
            open_stacks.append(es_ln2)
            ln2t = es_ln2.enter_context(
                nc.sbuf_tensor("ln2t", [128, 8 * 2048], BF16, side="right"))
            es_wo = ExitStack()
            open_stacks.append(es_wo)
            wo_sb = es_wo.enter_context(
                nc.sbuf_tensor("wo_sb", [128, 8 * 1024], BF16, side="right"))
            dma("gpsimd", wo_sb[:].rearrange("p (f c) -> p f c", f=8),
                d_wo.rearrange("(f p) c -> p f c", p=128))
            wo_l = sG.n
            dma("sync", x1t[:].rearrange("p (f c) -> p f c", f=8),
                d_xresT.rearrange("(f p) t -> p f t", p=128))
            xres_l = sX.n
            bo2_v = 0

            PO4 = [(SPa, 0), (SPa, 512), (SPb, 0), (SPb, 512)]
            o_evac = {}
            ln2v = 0
            k = 0
            for ts in range(4):
                for e in range(8):
                    PS, co = PO4[k % 4]
                    for g in range(8):
                        P.step("tensor",
                               lambda t, PS=PS, co=co, e=e, g=g, ts=ts: t.matmul(
                                   PS[:, co:co + 512],
                                   wo_sb[:, 1024 * g + 128 * e:
                                         1024 * g + 128 * e + 128],
                                   aot[:, 2048 * g + 512 * ts:
                                       2048 * g + 512 * ts + 512],
                                   start=(g == 0), stop=(g == 7)),
                               waits=[(sG, wo_l), (sV, o_evac.get(k - 4, 0)),
                                      (sX, pro_x)],
                               incs=([(sT, 1)] if g == 7 else []),
                               force=(g == 0))
                    mm_t = sT.n
                    dcol = 2048 * e + 512 * ts
                    P.step("vector",
                           lambda v, PS=PS, co=co, dcol=dcol, e=e:
                           v.scalar_tensor_tensor(
                               x1t[:, dcol:dcol + 512], PS[:, co:co + 512],
                               bo_sb[:, e:e + 1], x1t[:, dcol:dcol + 512],
                               OP.add, OP.add),
                           waits=[(sT, mm_t), (sX, xres_l)], incs=[(sV, 1)])
                    o_evac[k] = sV.n
                    k += 1
                if STOP_AFTER != "oproj":
                    for k8 in (2 * ts, 2 * ts + 1):
                        t0 = 256 * k8
                        d3 = ln2t[:].rearrange("p (f c) -> p f c", c=2048)[
                            :, :, t0:t0 + 256]
                        ln2v = emit_ln_strip((x1t, 0), t0, d3, (sS, sS.n))
            P.barrier(SEMS)
            if DEBUG:
                dma("gpsimd", taps["t_aot"][:, :], aot[:])
                dma("sync", taps["t_x1"][:, :], x1t[:])
                dma("gpsimd", taps["t_wo"][:, :], wo_sb[:])
                P.barrier(SEMS)
            es_wo.close()
            open_stacks.remove(es_wo)
            es_aot.close()
            open_stacks.remove(es_aot)

            if STOP_AFTER != "oproj":
                for f in range(8):
                    P.step("vector", lambda v, f=f: v.tensor_scalar_add(
                        x1t[:, 2048 * f:2048 * f + 2048],
                        x1t[:, 2048 * f:2048 * f + 2048], b2_sb[:, f:f + 1]),
                        waits=[(sV, ln2v)],
                        incs=([(sV, 1)] if f == 7 else []))
                P.barrier(SEMS)
                if DEBUG:
                    dma("gpsimd", taps["t_ln2"][:, :], ln2t[:]),
                    P.barrier(SEMS)

                if STOP_AFTER != "ln2":
                    es_ffn = ExitStack()
                    open_stacks.append(es_ffn)
                    w1_sb = es_ffn.enter_context(
                        nc.sbuf_tensor("w1_sb", [128, 8 * 1024], BF16,
                                       side="right"))
                    w2_sb = es_ffn.enter_context(
                        nc.sbuf_tensor("w2_sb", [128, 8 * 1024], BF16,
                                       side="right"))
                    ht = es_ffn.enter_context(
                        nc.sbuf_tensor("ht", [128, 8 * 2048], BF16, side="right"))
                    PSLOT = [(SPa, 0), (SPa, 512), (SPb, 0), (SPb, 512),
                             (SPc, 0), (SPc, 512), (POa, 0), (POb, 0)]
                    h_done_t = 0
                    out_done_t = 0
                    out_evac_all = 0
                    for q in range(4):
                        dma("gpsimd", w1_sb[:].rearrange("p (f c) -> p f c", f=8),
                            d_w1.rearrange("(f p) c -> p f c", p=128)[
                                :, :, 1024 * q:1024 * q + 1024],
                            waits=[(sT, h_done_t)])
                        w1q_l = sG.n
                        dma("gpsimd", w2_sb[:].rearrange("p (f c) -> p f c", f=8),
                            d_w2.rearrange("(f p) c -> p f c", p=128)[
                                :, 8 * q:8 * q + 8, :],
                            waits=[(sT, out_done_t)])
                        w2q_l = sG.n
                        h_evac = {}
                        for m in range(8):
                            slots = PSLOT[0:4] if m % 2 == 0 else PSLOT[4:8]
                            for f in range(8):
                                for ts in range(4):
                                    PS, co = slots[ts]
                                    P.step("tensor",
                                           lambda t, PS=PS, co=co, f=f, m=m, ts=ts:
                                           t.matmul(
                                               PS[:, co:co + 512],
                                               w1_sb[:, 1024 * f + 128 * m:
                                                     1024 * f + 128 * m + 128],
                                               ln2t[:, 2048 * f + 512 * ts:
                                                    2048 * f + 512 * ts + 512],
                                               start=(f == 0), stop=(f == 7)),
                                           waits=[(sG, w1q_l),
                                                  (sS, h_evac.get(m - 2, 0)),
                                                  (sV, out_evac_all)],
                                           incs=([(sT, 1)]
                                                 if (f == 7 and ts == 3) else []),
                                           force=(ts == 0))
                            mm_t = sT.n
                            for ts in range(4):
                                PS, co = slots[ts]
                                P.step("scalar",
                                       lambda s, PS=PS, co=co, m=m, q=q, ts=ts:
                                       s.activation(
                                           ht[:, 2048 * m + 512 * ts:
                                              2048 * m + 512 * ts + 512],
                                           PS[:, co:co + 512], AF.Relu,
                                           bias=b1_sb[:, 8 * q + m:8 * q + m + 1]),
                                       waits=[(sT, mm_t), (sT, out_done_t)],
                                       incs=([(sS, 1)] if ts == 3 else []))
                            h_evac[m] = sS.n
                        h_done_t = sT.n
                        h_all_s = sS.n
                        oev = {}
                        for e in range(8):
                            slots = PSLOT[0:4] if e % 2 == 0 else PSLOT[4:8]
                            for hid in range(8):
                                for ts in range(4):
                                    PS, co = slots[ts]
                                    P.step("tensor",
                                           lambda t, PS=PS, co=co, e=e, hid=hid,
                                           ts=ts: t.matmul(
                                               PS[:, co:co + 512],
                                               w2_sb[:, 1024 * hid + 128 * e:
                                                     1024 * hid + 128 * e + 128],
                                               ht[:, 2048 * hid + 512 * ts:
                                                  2048 * hid + 512 * ts + 512],
                                               start=(hid == 0), stop=(hid == 7)),
                                           waits=[(sG, w2q_l), (sS, h_all_s),
                                                  (sV, oev.get(e - 2, 0))],
                                           incs=([(sT, 1)]
                                                 if (hid == 7 and ts == 3) else []),
                                           force=(ts == 0))
                            mm_t = sT.n
                            for ts in range(4):
                                PS, co = slots[ts]
                                dcol = 2048 * e + 512 * ts
                                P.step("vector",
                                       lambda v, PS=PS, co=co, dcol=dcol:
                                       v.tensor_tensor(
                                           x1t[:, dcol:dcol + 512],
                                           x1t[:, dcol:dcol + 512],
                                           PS[:, co:co + 512], OP.add),
                                       waits=[(sT, mm_t)],
                                       incs=([(sV, 1)] if ts == 3 else []))
                            oev[e] = sV.n
                        out_done_t = sT.n
                        out_evac_all = sV.n
                    es_ffn.close()
                    open_stacks.remove(es_ffn)
                    es_ln2.close()
                    open_stacks.remove(es_ln2)
            P.barrier(SEMS)
            dma("sync", d_outT.rearrange("(f p) t -> p f t", p=128),
                x1t[:].rearrange("p (f c) -> p f c", f=8))
        else:
            with nc.sbuf_tensor("zro", [128, 2048], F32, side="right") as zro:
                P.step("vector", lambda v: v.memset(zro[:], 0.0), incs=[(sV, 1)])
                zv = sV.n
                for f in range(8):
                    dma("sync", d_outT[128 * f:128 * f + 128, :], zro[:, :],
                        waits=[(sV, zv)])
        for st_ in reversed(open_stacks):
            try:
                st_.close()
            except Exception:
                pass
        P.step("sync", lambda e: None, waits=[(sX, sX.n), (sG, sG.n)])

        blk = es.enter_context(nc.Block())

        def _run(name):
            def body(eng, _n=name):
                for fn in P.steps[_n]:
                    fn(eng)
            return body
        blk.tensor(_run("tensor"))
        blk.vector(_run("vector"))
        blk.scalar(_run("scalar"))
        blk.gpsimd(_run("gpsimd"))
        blk.sync(_run("sync"))
    return nc


def _host_tables(p, rel_bias):
    qi = np.arange(64)
    dmap = ({(0, 0): 0, (0, 1): 2, (1, 0): None, (1, 1): 1} if p == 0 else
            {(0, 0): 0, (0, 1): 2, (1, 0): 1, (1, 1): 3})
    idx = np.zeros((128, 128), np.int64)
    for jh in (0, 1):
        for rh in (0, 1):
            d = dmap[(jh, rh)]
            if d is None:
                blk = np.full((64, 64), 256, np.int64)
            else:
                blk = np.clip(64 * d + qi[None, :] - qi[:, None], -128, 128) + 128
            idx[64 * jh:64 * jh + 64, 64 * rh:64 * rh + 64] = blk
    btraw = np.ascontiguousarray(rel_bias[idx].transpose(2, 0, 1)).astype(np.float32)
    mask = np.zeros((128, 128), np.float32)
    if p == 0:
        mask[64:128, 0:64] = MASKVAL
    return btraw, mask


def kernel(L0_attn_x, L0_attn_Wq, L0_attn_bq, L0_attn_Wk, L0_attn_bk,
           L0_attn_Wv, L0_attn_bv, L0_attn_Wo, L0_attn_bo, L0_attn_rel_bias,
           L0_ffn_w1, L0_ffn_b1, L0_ffn_w2, L0_ffn_b2):
    f32 = np.float32
    x = np.asarray(L0_attn_x, f32)
    rel = np.asarray(L0_attn_rel_bias, f32)
    shared = {
        "wq": np.ascontiguousarray(L0_attn_Wq, f32),
        "wk": np.ascontiguousarray(L0_attn_Wk, f32),
        "wv": np.ascontiguousarray(L0_attn_Wv, f32),
        "wo": np.ascontiguousarray(L0_attn_Wo, f32),
        "w1": np.ascontiguousarray(L0_ffn_w1, f32),
        "w2": np.ascontiguousarray(L0_ffn_w2, f32),
        "bq": np.ascontiguousarray(np.asarray(L0_attn_bq, f32).reshape(8, 128).T),
        "bk": np.ascontiguousarray(np.asarray(L0_attn_bk, f32).reshape(8, 128).T),
        "bv": np.ascontiguousarray(np.asarray(L0_attn_bv, f32).reshape(8, 128).T),
        "bvrep": np.ascontiguousarray(
            np.broadcast_to(np.asarray(L0_attn_bv, f32)[None, :], (128, D))),
        "bo": np.ascontiguousarray(np.asarray(L0_attn_bo, f32).reshape(8, 128).T),
        "b1": np.ascontiguousarray(np.asarray(L0_ffn_b1, f32).reshape(32, 128).T),
        "b2": np.ascontiguousarray(np.asarray(L0_ffn_b2, f32).reshape(8, 128).T),
        "crep": np.ascontiguousarray(
            np.broadcast_to(rel[256][None, :], (128, H)).astype(f32)),
        "ones_a": np.ones((128, 1), BFNP),
        "ones_c": np.ones((1, 128), BFNP),
    }
    tables = {p: _host_tables(p, rel) for p in (0, 1)}
    in_maps = []
    for c in range(8):
        b, p = c >> 1, c & 1
        ch = x[b].reshape(64, 64, D)
        order = [j ^ p for j in range(64)]
        xkvT = np.ascontiguousarray(ch[order].reshape(S, D).T)
        own = [2 * r + p for r in range(32)]
        xresT = np.ascontiguousarray(ch[own].reshape(OWN, D).T)
        im = dict(shared)
        im["xkvT"] = xkvT
        im["xresT"] = xresT
        im["btraw"] = tables[p][0]
        im["btmask"] = tables[p][1]
        in_maps.append(im)
    nc = build_nc()
    res = run_bass_kernel_spmd(nc, in_maps, core_ids=list(range(8)))
    if TAP_SINK is not None:
        TAP_SINK.append(res.results)
    out = np.empty((4, S, D), f32)
    for c in range(8):
        b, p = c >> 1, c & 1
        oT = np.asarray(res.results[c]["outT"], f32)
        out[b].reshape(64, 64, D)[[2 * r + p for r in range(32)]] = (
            oT.T.reshape(32, 64, D))
    return out
